# version 11 (final)
# baseline (speedup 1.0000x reference)
import contextlib
import math
import numpy as np
import ml_dtypes
import concourse.bass as bass
import concourse.mybir as mybir
from concourse.bass_utils import run_bass_kernel_spmd

F32 = mybir.dt.float32
BF16 = mybir.dt.bfloat16
I32 = mybir.dt.int32
AF = mybir.ActivationFunctionType
ALU = mybir.AluOpType
AX = mybir.AxisListType
BF = ml_dtypes.bfloat16

D = 4096
T = 2048
S = 4096
NT = D // 128
NCH = T // 512
NTT = T // 128
HEADS = 32
IDIM = 11008
ITL = IDIM // 128
SCALE_ATT = 1.0 / math.sqrt(128.0)
LN_EPS = 1e-5
MEAN_CNT = 4095 - 611

ARO = [0, 4, 1, 5, 2, 6, 3, 7]
TPOS = [0] * 8
for _t, _i in enumerate(ARO):
    TPOS[_i] = _t


def r32(ap):
    return ap.rearrange("(t p) c -> t p c", p=128)


class _TileView:

    def __init__(self, sb, width):
        self.sb = sb
        self.width = width

    def __getitem__(self, i):
        return self.sb[:, self.width * i:self.width * (i + 1)]


def sb3(sb, width):
    return _TileView(sb, width)


def sbdma(sb, width, n, off=0):
    return sb[:, off:off + n * width].rearrange("p (t c) -> p t c", c=width)


def r32p(ap):
    return ap.rearrange("(t p) c -> p t c", p=128)


def sbdma_full(sb, width):
    n = sb.shape[1] // width
    return sb[:, 0:n * width].rearrange("p (t c) -> p t c", c=width)


def sbdma_range(sb, width, a, b):
    return sb[:, a:b].rearrange("p (t c) -> p t c", c=width)


def bcast_row_off(dram_ap, elem_off, nrow, ncol):
    return bass.AP(dram_ap.tensor, dram_ap.offset + elem_off,
                   [[0, nrow], [1, ncol]])


def ln_builder(nc, t, layer):
    xTp = r32p(t["xT"])
    xnTp = r32p(t["xnT"])
    nconst = 3 if layer == 0 else 2
    with contextlib.ExitStack() as _es:
        x_sb = _es.enter_context(nc.sbuf_tensor("ln_x", [128, NT * 512], F32))
        xb_sb = _es.enter_context(nc.sbuf_tensor("ln_xb", [128, NT * 512], BF16))
        xq_sb = _es.enter_context(nc.sbuf_tensor("ln_xq", [128, NT * 512], BF16))
        xn_sb = _es.enter_context(nc.sbuf_tensor("ln_xn", [128, NT * 512], BF16))
        ones_col = _es.enter_context(nc.sbuf_tensor("ln_on", [128, 1], BF16))
        ones_row = _es.enter_context(nc.sbuf_tensor("ln_or", [1, 128], F32))
        st_sb = _es.enter_context(nc.sbuf_tensor("ln_st", [1, 4096], F32))
        ms_sb = _es.enter_context(nc.sbuf_tensor("ln_ms", [128, NT], F32))
        mk_sb = _es.enter_context(nc.sbuf_tensor("ln_mk", [128, T], BF16))
        red_sb = _es.enter_context(nc.sbuf_tensor("ln_red", [128, 128], F32))
        eps_sb = _es.enter_context(nc.sbuf_tensor("ln_eps", [1, 1], F32))
        ps_s = _es.enter_context(nc.psum_tensor("ln_ps", [1, 512], F32))
        ps_q = _es.enter_context(nc.psum_tensor("ln_pq", [1, 512], F32))
        ps_a = _es.enter_context(nc.psum_tensor("ln_pa", [128, 512], F32))
        ps_b = _es.enter_context(nc.psum_tensor("ln_pb", [128, 512], F32))
        ld = _es.enter_context(nc.semaphore("ln_ld"))
        mm = _es.enter_context(nc.semaphore("ln_mm"))
        sc = _es.enter_context(nc.semaphore("ln_sc"))
        ve = _es.enter_context(nc.semaphore("ln_ve"))
        so = _es.enter_context(nc.semaphore("ln_so"))
        lvv = _es.enter_context(nc.semaphore("ln_vv"))
        blk = _es.enter_context(nc.Block())
        xv = sb3(x_sb, 512)
        xbv = sb3(xb_sb, 512)
        xqv = sb3(xq_sb, 512)
        xnv = sb3(xn_sb, 512)
        MEAN, ESQ, M2, VAR = (slice(0, 512), slice(512, 1024),
                              slice(1024, 1536), slice(1536, 2048))
        SQV, RSTD, BLN = slice(2048, 2560), slice(2560, 3072), slice(3072, 3584)

        @blk.gpsimd
        def _(g):
            g.dma_start(ones_col[:], t["ones_col"]).then_inc(ld, 16)
            g.dma_start(ones_row[:], t["ones_row_f32"]).then_inc(ld, 16)
            if layer == 0:
                g.dma_start(mk_sb[:], t["maskln"]).then_inc(ld, 16)
            for ch in range(NCH):
                if ch > 0:
                    g.wait_ge(sc, 2 * (ch - 1) + 1)
                g.dma_start(
                    sbdma_full(x_sb, 512),
                    xTp[:, :, 512 * ch:512 * ch + 512],
                ).then_inc(ld, 16)

        @blk.scalar
        def _(s):
            for ch in range(NCH):
                s.wait_ge(ld, 16 * (nconst + ch + 1))
                if ch > 0:
                    s.wait_ge(ve, 3 * ch)
                for tt in range(NT):
                    s.copy(xbv[tt], xv[tt])
                    ins = s.activation(xqv[tt], xv[tt], AF.Square)
                ins.then_inc(sc, 1)
                s.wait_ge(ve, 3 * ch + 1)
                s.activation(st_sb[0:1, SQV], st_sb[0:1, VAR],
                             AF.Sqrt, bias=eps_sb[0:1, 0:1],
                             scale=1.0).then_inc(sc, 1)

        @blk.tensor
        def _(te):
            for ch in range(NCH):
                te.wait_ge(sc, 2 * ch + 1)
                if ch > 0:
                    te.wait_ge(ve, 3 * (ch - 1) + 1)
                for tt in range(NT):
                    te.matmul(ps_s[:], ones_col[:], xbv[tt],
                              start=(tt == 0), stop=(tt == NT - 1))
                    ins = te.matmul(ps_q[:], ones_col[:], xqv[tt],
                                    start=(tt == 0), stop=(tt == NT - 1))
                ins.then_inc(mm, 1)
                te.wait_ge(ve, 3 * ch + 2)
                te.matmul(ps_a[:], ones_row[:], st_sb[0:1, RSTD],
                          start=True, stop=True)
                te.matmul(ps_b[:], ones_row[:], st_sb[0:1, BLN],
                          start=True, stop=True).then_inc(mm, 1)

        @blk.vector
        def _(v):
            v.memset(eps_sb[:], LN_EPS)
            for ch in range(NCH):
                v.wait_ge(mm, 2 * ch + 1)
                v.tensor_scalar(st_sb[0:1, MEAN], ps_s[:], 1.0 / D, None, ALU.mult)
                v.tensor_scalar(st_sb[0:1, ESQ], ps_q[:], 1.0 / D, None, ALU.mult)
                v.tensor_tensor(st_sb[0:1, M2], st_sb[0:1, MEAN],
                                st_sb[0:1, MEAN], ALU.mult).then_inc(lvv, 1)
                v.wait_ge(lvv, 2 * ch + 1)
                v.tensor_sub(st_sb[0:1, VAR], st_sb[0:1, ESQ],
                             st_sb[0:1, M2]).then_inc(ve, 1)
                v.wait_ge(sc, 2 * ch + 2)
                v.reciprocal(st_sb[0:1, RSTD],
                             st_sb[0:1, SQV]).then_inc(lvv, 1)
                v.wait_ge(lvv, 2 * ch + 2)
                v.scalar_tensor_tensor(st_sb[0:1, BLN], st_sb[0:1, MEAN], -1.0,
                                       st_sb[0:1, RSTD], ALU.mult,
                                       ALU.mult).then_inc(ve, 1)
                v.wait_ge(mm, 2 * ch + 2)
                if ch > 0:
                    v.wait_ge(so, 16 * ch)
                for tt in range(0, NT, 2):
                    v.tensor_tensor(xnv[tt], xbv[tt], ps_a[:], ALU.mult)
                    v.tensor_tensor(xnv[tt + 1], xbv[tt + 1], ps_a[:], ALU.mult)
                    v.tensor_tensor(xnv[tt], xnv[tt], ps_b[:], ALU.add)
                    ins = v.tensor_tensor(xnv[tt + 1], xnv[tt + 1], ps_b[:],
                                          ALU.add)
                if layer == 0:
                    for tt in range(NT):
                        v.tensor_tensor(xqv[tt], xnv[tt],
                                        mk_sb[:, 512 * ch:512 * ch + 512],
                                        ALU.mult)
                    for tt in range(NT):
                        ins = v.reduce_sum(red_sb[:, NT * ch + tt:
                                                  NT * ch + tt + 1],
                                           xqv[tt], AX.X)
                    if ch == NCH - 1:
                        base = 2 * NCH
                        ins.then_inc(lvv, 1)
                        v.wait_ge(lvv, base + 1)
                        v.tensor_add(ms_sb[:], red_sb[:, 0:NT],
                                     red_sb[:, NT:2 * NT]).then_inc(lvv, 1)
                        v.wait_ge(lvv, base + 2)
                        v.tensor_add(ms_sb[:], ms_sb[:],
                                     red_sb[:, 2 * NT:3 * NT]).then_inc(lvv, 1)
                        v.wait_ge(lvv, base + 3)
                        ins = v.tensor_add(ms_sb[:], ms_sb[:],
                                           red_sb[:, 3 * NT:4 * NT])
                ins.then_inc(ve, 1)

        @blk.sync
        def _(sy):
            for ch in range(NCH):
                sy.wait_ge(ve, 3 * ch + 3)
                sy.dma_start(
                    xnTp[:, :, 512 * ch:512 * ch + 512],
                    sbdma_full(xn_sb, 512),
                ).then_inc(so, 16)
            if layer == 0:
                sy.dma_start(t["msum"], ms_sb[:]).then_inc(so, 16)
            sy.wait_ge(so, 16 * (NCH + (1 if layer == 0 else 0)))


def qkv_builder(nc, t, layer):
    rot = layer == 1
    NIT = 2 * NT * NCH
    with contextlib.ExitStack() as _es:
        xn_sb = _es.enter_context(nc.sbuf_tensor("qk_xn", [128, NT * T], BF16))
        ps0 = _es.enter_context(nc.psum_tensor("qk_p0", [128, 512], F32))
        ps1 = _es.enter_context(nc.psum_tensor("qk_p1", [128, 512], F32))
        pj0 = _es.enter_context(nc.psum_tensor("qk_j0", [128, 512], F32))
        pj1 = _es.enter_context(nc.psum_tensor("qk_j1", [128, 512], F32))
        ld = _es.enter_context(nc.semaphore("qk_ld"))
        lx = _es.enter_context(nc.semaphore("qk_lx"))
        mm = _es.enter_context(nc.semaphore("qk_mm"))
        sc = _es.enter_context(nc.semaphore("qk_sc"))
        jm = _es.enter_context(nc.semaphore("qk_jm"))
        ve = _es.enter_context(nc.semaphore("qk_ve"))
        so = _es.enter_context(nc.semaphore("qk_so"))
        vld = _es.enter_context(nc.semaphore("qk_vld"))
        vmm = _es.enter_context(nc.semaphore("qk_vmm"))
        vsc = _es.enter_context(nc.semaphore("qk_vsc"))
        vso = _es.enter_context(nc.semaphore("qk_vso"))
        vq = _es.enter_context(nc.semaphore("qk_vq"))
        psl = [ps0, ps1]
        pjl = [pj0, pj1]

        def xn_t(k, ch):
            return xn_sb[:, T * k + 512 * ch: T * k + 512 * ch + 512]

        def xn_tile(k, tt):
            return xn_sb[:, T * k + 128 * tt: T * k + 128 * tt + 128]

        names = [("wqT", "qT"), ("wkT", "kown")]
        with contextlib.ExitStack() as _e1:
            w_sb = _e1.enter_context(
                nc.sbuf_tensor("qk_w", [128, 2 * NT * 128], BF16))
            ev_sb = _e1.enter_context(
                nc.sbuf_tensor("qk_ev", [128, 2 * 512], BF16))
            jt_sb = _e1.enter_context(
                nc.sbuf_tensor("qk_jt", [128, 128], BF16))
            cs_sb = _e1.enter_context(
                nc.sbuf_tensor("qk_cs", [128, 2 * T], BF16))
            r1_sb = _e1.enter_context(
                nc.sbuf_tensor("qk_r1", [128, 2 * 512], BF16))
            r2_sb = _e1.enter_context(
                nc.sbuf_tensor("qk_r2", [128, 2 * 512], BF16))
            wv = sb3(w_sb, 128)
            evv = sb3(ev_sb, 512)
            r1v = sb3(r1_sb, 512)
            r2v = sb3(r2_sb, 512)
            with nc.Block() as blk:
                @blk.gpsimd
                def _(g):
                    for wi in range(2):
                        wsrc = r32p(t[names[wi][0]])
                        for nt in range(NT):
                            i = wi * NT + nt
                            if i >= 2:
                                g.wait_ge(mm, (i - 1) * NCH)
                            g.dma_start(
                                sbdma(w_sb, 128, NT, (i % 2) * NT * 128),
                                wsrc[:, :, 128 * nt:128 * nt + 128],
                            ).then_inc(ld, 16)

                @blk.tensor
                def _(te):
                    te.wait_ge(lx, 16 * (4 if rot else 1))
                    for it in range(NIT):
                        i, ch = it // NCH, it % NCH
                        par = it % 2
                        if ch == 0:
                            te.wait_ge(ld, 16 * (i + 1))
                        if it >= 2:
                            te.wait_ge(sc, it - 1)
                        for k in range(NT):
                            ins = te.matmul(psl[par][:], wv[(i % 2) * NT + k],
                                            xn_t(k, ch), start=(k == 0),
                                            stop=(k == NT - 1))
                        ins.then_inc(mm, 1)
                        if rot and it >= 1:
                            j = it - 1
                            te.wait_ge(sc, j + 1)
                            if j >= 2:
                                te.wait_ge(ve, j - 1)
                            te.matmul(pjl[j % 2][:], jt_sb[:], evv[j % 2],
                                      start=True, stop=True).then_inc(jm, 1)
                    if rot:
                        j = NIT - 1
                        te.wait_ge(sc, j + 1)
                        te.wait_ge(ve, j - 1)
                        te.matmul(pjl[j % 2][:], jt_sb[:], evv[j % 2],
                                  start=True, stop=True).then_inc(jm, 1)

                @blk.scalar
                def _(s):
                    for it in range(NIT):
                        par = it % 2
                        s.wait_ge(mm, it + 1)
                        if it >= 2:
                            if rot:
                                s.wait_ge(ve, it - 1)
                            else:
                                s.wait_ge(so, 16 * (it - 1))
                        s.copy(evv[par], psl[par][:]).then_inc(sc, 1)

                @blk.vector
                def _(v):
                    if rot:
                        for it in range(NIT):
                            par = it % 2
                            ch = it % NCH
                            v.wait_ge(jm, it + 1)
                            if it >= 2:
                                v.wait_ge(so, 16 * (it - 1))
                            v.tensor_tensor(
                                r2v[par], pjl[par][:],
                                cs_sb[:, T + 512 * ch:T + 512 * ch + 512],
                                ALU.mult)
                            v.tensor_tensor(r1v[par], evv[par],
                                            cs_sb[:, 512 * ch:512 * ch + 512],
                                            ALU.mult).then_inc(vq, 1)
                            v.wait_ge(vq, it + 1)
                            v.tensor_add(r1v[par], r1v[par],
                                         r2v[par]).then_inc(ve, 1)

                @blk.sync
                def _(sy):
                    sy.dma_start(sbdma_full(xn_sb, T),
                                 r32p(t["xnT"])).then_inc(lx, 16)
                    if rot:
                        sy.dma_start(jt_sb[:], t["JT"]).then_inc(lx, 16)
                        sy.dma_start(cs_sb[:, 0:T], t["cosb"]).then_inc(lx, 16)
                        sy.dma_start(cs_sb[:, T:2 * T],
                                     t["sinb"]).then_inc(lx, 16)
                    for it in range(NIT):
                        par = it % 2
                        i, ch = it // NCH, it % NCH
                        on = names[i // NT][1]
                        nt = i % NT
                        dst = r32(t[on])[nt, :, 512 * ch:512 * ch + 512]
                        if rot:
                            sy.wait_ge(ve, it + 1)
                            sy.dma_start(dst, r1v[par]).then_inc(so, 16)
                        else:
                            sy.wait_ge(sc, it + 1)
                            sy.dma_start(dst, evv[par]).then_inc(so, 16)
                    sy.wait_ge(so, 16 * NIT)

        with contextlib.ExitStack() as _e2:
            wvc_sb = _e2.enter_context(
                nc.sbuf_tensor("qk_wv", [128, 2 * NT * 512], BF16))
            vev_sb = _e2.enter_context(
                nc.sbuf_tensor("qk_vev", [128, 2 * 512], BF16))
            wvv = sb3(wvc_sb, 512)
            vev = sb3(vev_sb, 512)
            with nc.Block() as blk2:
                @blk2.gpsimd
                def _(g):
                    wsrc = r32p(t["wvT"])
                    for dv in range(8):
                        if dv >= 2:
                            g.wait_ge(vmm, (dv - 1) * NTT)
                        g.dma_start(
                            sbdma(wvc_sb, 512, NT, (dv % 2) * NT * 512),
                            wsrc[:, :, 512 * dv:512 * dv + 512],
                        ).then_inc(vld, 16)

                @blk2.tensor
                def _(te):
                    for dv in range(8):
                        te.wait_ge(vld, 16 * (dv + 1))
                        for tt in range(NTT):
                            it = dv * NTT + tt
                            par = it % 2
                            if it >= 2:
                                te.wait_ge(vsc, it - 1)
                            for k in range(NT):
                                ins = te.matmul(psl[par][:], xn_tile(k, tt),
                                                wvv[(dv % 2) * NT + k],
                                                start=(k == 0),
                                                stop=(k == NT - 1))
                            ins.then_inc(vmm, 1)

                @blk2.scalar
                def _(s):
                    for it in range(8 * NTT):
                        par = it % 2
                        s.wait_ge(vmm, it + 1)
                        if it >= 2:
                            s.wait_ge(vso, 16 * (it - 1))
                        s.copy(vev[par], psl[par][:]).then_inc(vsc, 1)

                @blk2.sync
                def _(sy):
                    vdst = r32(t["vown"])
                    for it in range(8 * NTT):
                        par = it % 2
                        dv, tt = it // NTT, it % NTT
                        sy.wait_ge(vsc, it + 1)
                        sy.dma_start(vdst[tt, :, 512 * dv:512 * dv + 512],
                                     vev[par]).then_inc(vso, 16)
                    sy.wait_ge(vso, 16 * 8 * NTT)


def ffn0_builder(nc, t):
    with contextlib.ExitStack() as _es:
        xn_sb = _es.enter_context(nc.sbuf_tensor("f0_xn", [128, NT * T], BF16))
        w_sb = _es.enter_context(nc.sbuf_tensor("f0_w", [128, 2 * NT * 128], BF16))
        c_sb = _es.enter_context(nc.sbuf_tensor("f0_c", [128, NT * 128], BF16))
        a_sb = _es.enter_context(nc.sbuf_tensor("f0_a", [128, NT * 128], BF16))
        tp_sb = _es.enter_context(nc.sbuf_tensor("f0_tp", [128, NCH * 512], BF16))
        ms_sb = _es.enter_context(nc.sbuf_tensor("f0_ms", [128, 2 * NT], F32))
        rw_sb = _es.enter_context(nc.sbuf_tensor("f0_rw", [128, NT * 8], F32))
        b_sb = _es.enter_context(nc.sbuf_tensor("f0_b", [128, NT], F32))
        st_sb = _es.enter_context(nc.sbuf_tensor("f0_st", [1, 64], F32))
        rep_sb = _es.enter_context(nc.sbuf_tensor("f0_rep", [128, 1], F32))
        ev_sb = _es.enter_context(nc.sbuf_tensor("f0_ev", [128, 2 * 512], BF16))
        ps0 = _es.enter_context(nc.psum_tensor("f0_p0", [128, 512], F32))
        ps1 = _es.enter_context(nc.psum_tensor("f0_p1", [128, 512], F32))
        ps_l = _es.enter_context(nc.psum_tensor("f0_pl", [1, 8], F32))
        ld = _es.enter_context(nc.semaphore("f0_ld"))
        mm = _es.enter_context(nc.semaphore("f0_mm"))
        sc = _es.enter_context(nc.semaphore("f0_sc"))
        ve = _es.enter_context(nc.semaphore("f0_ve"))
        so = _es.enter_context(nc.semaphore("f0_so"))
        lh2 = _es.enter_context(nc.semaphore("f0_lh2"))
        lw2 = _es.enter_context(nc.semaphore("f0_lw2"))
        mm2 = _es.enter_context(nc.semaphore("f0_mm2"))
        sc2 = _es.enter_context(nc.semaphore("f0_sc2"))
        ve2 = _es.enter_context(nc.semaphore("f0_ve2"))
        so2 = _es.enter_context(nc.semaphore("f0_so2"))
        vv = _es.enter_context(nc.semaphore("f0_vv"))
        dbs = _es.enter_context(nc.semaphore("f0_dbs"))
        wv = sb3(w_sb, 128)
        cv = sb3(c_sb, 128)
        av = sb3(a_sb, 128)
        evv = sb3(ev_sb, 512)
        rwv = sb3(rw_sb, 8)
        psl = [ps0, ps1]
        L0_, L8, L16 = slice(0, 8), slice(8, 16), slice(16, 24)
        RB, NM, SM, RC = (slice(32, 40), slice(40, 41), slice(41, 42),
                          slice(42, 43))

        def xn_t(k, ch):
            return xn_sb[:, T * k + 512 * ch: T * k + 512 * ch + 512]

        with nc.Block() as blk:
            @blk.sync
            def _(sy):
                sy.dma_start(ms_sb[:, 0:NT], t["msum_a"]).then_inc(ld, 16)
                sy.dma_start(ms_sb[:, NT:2 * NT], t["msum_b"]).then_inc(ld, 16)
                sy.dma_start(sbdma_full(rw_sb, 8),
                             r32p(t["rwT"])).then_inc(ld, 16)
                sy.dma_start(st_sb[0:1, RB], t["rb"]).then_inc(ld, 16)
                sy.wait_ge(ve, 2)
                sy.dma_start(t["rt8"], st_sb[0:1, L16]).then_inc(so, 16)
                sy.wait_ge(so, 16)
                sy.dma_start(
                    rep_sb[:],
                    bass.AP(t["rt8"].tensor, t["rt8"].offset, [[1, 8], [0, 16]]),
                    allow_slow_non_contiguous=True,
                ).then_inc(ld, 16)

            @blk.vector
            def _(v):
                v.memset(st_sb[0:1, 0:64], 0.0)
                v.wait_ge(ld, 16 * 4)
                v.wait_ge(mm, 1)
                v.tensor_scalar(st_sb[0:1, L0_], ps_l[:], 1.0 / MEAN_CNT, None,
                                ALU.mult).then_inc(vv, 1)
                v.wait_ge(vv, 1)
                v.tensor_add(st_sb[0:1, L0_], st_sb[0:1, L0_],
                             st_sb[0:1, RB]).then_inc(vv, 1)
                v.wait_ge(vv, 2)
                v.reduce_max(st_sb[0:1, NM], st_sb[0:1, L0_], AX.X,
                             negate=True).then_inc(ve, 1)
                v.wait_ge(sc, 1)
                v.reduce_sum(st_sb[0:1, SM], st_sb[0:1, L8],
                             AX.X).then_inc(vv, 1)
                v.wait_ge(vv, 3)
                v.reciprocal(st_sb[0:1, RC], st_sb[0:1, SM]).then_inc(vv, 1)
                v.wait_ge(vv, 4)
                v.tensor_scalar(st_sb[0:1, L16], st_sb[0:1, L8],
                                st_sb[0:1, RC], None, ALU.mult).then_inc(ve, 1)


            @blk.scalar
            def _(s):
                s.wait_ge(ve, 1)
                s.activation(st_sb[0:1, L8], st_sb[0:1, L0_], AF.Exp,
                             bias=st_sb[0:1, NM], scale=1.0).then_inc(sc, 1)


            @blk.tensor
            def _(te):
                te.wait_ge(ld, 16 * 3)
                for k in range(2 * NT):
                    ins = te.matmul(ps_l[:], ms_sb[:, k:k + 1], rwv[k % NT],
                                    start=(k == 0), stop=(k == 2 * NT - 1))
                ins.then_inc(mm, 1)


        with nc.Block() as blk2:
            @blk2.sync
            def _(sy):
                sy.dma_start(sbdma_full(xn_sb, T),
                             r32p(t["xnT"])).then_inc(lh2, 16)
                sy.dma_start(b_sb[:], t["b0rs"]).then_inc(lh2, 16)
                for it in range(NT * NCH):
                    par = it % 2
                    nt, ch = it // NCH, it % NCH
                    sy.wait_ge(sc2, it + 1)
                    sy.dma_start(r32(t["gT"])[nt, :, 512 * ch:512 * ch + 512],
                                 evv[par]).then_inc(so2, 16)
                sy.wait_ge(so2, 16 * NT * NCH)

            @blk2.gpsimd
            def _(g):
                g.dma_start(sbdma_full(a_sb, 128),
                            r32p(t["A_T"])).then_inc(lw2, 16)
                g.dma_start(sbdma_full(c_sb, 128),
                            r32p(t["C0"])).then_inc(lw2, 16)
                wsrc = r32p(t["W0T"])
                for nt in range(NT):
                    if nt >= 2:
                        g.wait_ge(mm2, NCH + (nt - 1) * NCH)
                    g.dma_start(
                        sbdma(w_sb, 128, NT, (nt % 2) * NT * 128),
                        wsrc[:, :, 128 * nt:128 * nt + 128],
                    ).then_inc(lw2, 16)
                if "dbgt" in t:
                    g.wait_ge(sc2, NT * NCH)
                    g.dma_start(t["dbgst"][0:1, 0:43],
                                st_sb[0:1, 0:43]).then_inc(dbs, 16)
                    g.dma_start(t["dbgt"][:, 0:NCH * 512],
                                tp_sb[:]).then_inc(dbs, 16)
                    g.dma_start(t["dbgt"][:, NCH * 512:NCH * 512 + 1],
                                rep_sb[:],
                                allow_slow_non_contiguous=True).then_inc(dbs, 16)
                    g.wait_ge(dbs, 48)

            @blk2.tensor
            def _(te):
                te.wait_ge(lh2, 16)
                te.wait_ge(lw2, 16)
                for ch in range(NCH):
                    par = ch % 2
                    if ch >= 2:
                        te.wait_ge(ve2, ch - 1)
                    for k in range(NT):
                        ins = te.matmul(psl[par][:], av[k], xn_t(k, ch),
                                        start=(k == 0), stop=(k == NT - 1))
                    ins.then_inc(mm2, 1)
                te.wait_ge(lw2, 32)
                te.wait_ge(ve2, NCH)
                for it in range(NT * NCH):
                    nt, ch = it // NCH, it % NCH
                    par = it % 2
                    if ch == 0:
                        te.wait_ge(lw2, 32 + 16 * (nt + 1))
                    if it >= 2:
                        te.wait_ge(sc2, it - 1)
                    for k in range(NT):
                        te.matmul(psl[par][:], wv[(nt % 2) * NT + k],
                                  xn_t(k, ch), start=(k == 0), stop=False)
                    te.matmul(psl[par][:], cv[nt],
                              tp_sb[:, 512 * ch:512 * ch + 512],
                              start=False, stop=True).then_inc(mm2, 1)

            @blk2.vector
            def _(v):
                v.wait_ge(ld, 16 * 5)
                for ch in range(NCH):
                    v.wait_ge(mm2, ch + 1)
                    v.tensor_scalar(tp_sb[:, 512 * ch:512 * ch + 512],
                                    psl[ch % 2][:], rep_sb[:], 2.0,
                                    ALU.mult, ALU.mult).then_inc(ve2, 1)

            @blk2.scalar
            def _(s):
                s.wait_ge(lh2, 32)
                for it in range(NT * NCH):
                    par = it % 2
                    nt = it // NCH
                    s.wait_ge(mm2, NCH + it + 1)
                    if it >= 2:
                        s.wait_ge(so2, 16 * (it - 1))
                    s.activation(evv[par], psl[par][:], AF.Silu,
                                 bias=b_sb[:, nt:nt + 1],
                                 scale=1.0).then_inc(sc2, 1)


def deq_builder(nc, t):
    mats = [
        ("g", D, IDIM, "gqw", "gsc", "gqz", "wdg", "zsg"),
        ("u", D, IDIM, "uqw", "usc", "uqz", "wdu", "zsu"),
        ("d", IDIM, D, "dqw", "dsc", "dqz", "wdd", "zsd"),
    ]
    with contextlib.ExitStack() as _es:
        ld = _es.enter_context(nc.semaphore("dq_l"))
        ve = _es.enter_context(nc.semaphore("dq_v"))
        so = _es.enter_context(nc.semaphore("dq_s"))
        ld2 = _es.enter_context(nc.semaphore("dq_l2"))
        ve2 = _es.enter_context(nc.semaphore("dq_v2"))
        so2 = _es.enter_context(nc.semaphore("dq_s2"))
        dvq = _es.enter_context(nc.semaphore("dq_vq"))
        mi = [0]
        l2n = [0]
        v2n = [0]
        s2n = [0]
        for (mn, KD, ND, qwn, scn, qzn, wdn, zsn) in mats:
            G = KD // 128
            NP = ND // 8
            with contextlib.ExitStack() as _p:
                sc_sb = _p.enter_context(
                    nc.sbuf_tensor(f"dqsc{mn}", [128, ND], F32))
                sa_sb = _p.enter_context(
                    nc.sbuf_tensor(f"dqsa{mn}", [128, ND], BF16))
                qz_sb = _p.enter_context(
                    nc.sbuf_tensor(f"dqqz{mn}", [128, NP], I32))
                zn_sb = _p.enter_context(
                    nc.sbuf_tensor(f"dqzn{mn}", [128, ND], I32))
                zs_sb = _p.enter_context(
                    nc.sbuf_tensor(f"dqzs{mn}", [128, ND], BF16))
                with nc.Block() as blk:
                    @blk.sync
                    def _(sy, G=G, NP=NP, ND=ND, scn=scn, qzn=qzn,
                          sc_sb=sc_sb, qz_sb=qz_sb):
                        sy.dma_start(sc_sb[0:G, 0:ND], t[scn]).then_inc(ld, 16)
                        sy.dma_start(qz_sb[0:G, 0:NP], t[qzn]).then_inc(ld, 16)

                    @blk.vector
                    def _(v, G=G, ND=ND, NP=NP, base=mi[0], sc_sb=sc_sb,
                          sa_sb=sa_sb, qz_sb=qz_sb, zn_sb=zn_sb, zs_sb=zs_sb):
                        v.wait_ge(ld, base * 32 + 32)
                        for i in range(8):
                            fac = 1.0 if i == 7 else 2.0 ** (-4 * i)
                            v.tensor_single_scalar(
                                sa_sb[0:G, TPOS[i]:ND:8],
                                sc_sb[0:G, TPOS[i]:ND:8], fac, ALU.mult)
                        qz = qz_sb[0:G, 0:NP]
                        for i in range(7):
                            v.tensor_single_scalar(
                                zn_sb[0:G, TPOS[i]:ND:8], qz, 15 << (4 * i),
                                ALU.bitwise_and)
                        v.tensor_single_scalar(
                            zn_sb[0:G, TPOS[7]:ND:8], qz, 28,
                            ALU.logical_shift_right).then_inc(ve, 1)
                        v.wait_ge(ve, 2 * base + 1)
                        v.scalar_tensor_tensor(
                            zs_sb[0:G, 0:ND], zn_sb[0:G, 0:ND], -1.0,
                            sa_sb[0:G, 0:ND], ALU.mult,
                            ALU.mult).then_inc(ve, 1)

                    @blk.gpsimd
                    def _(g, G=G, ND=ND, mn=mn, zsn=zsn, base=mi[0],
                          sa_sb=sa_sb, zs_sb=zs_sb):
                        g.wait_ge(ve, 2 * base + 2)
                        g.dma_start(t[f"sa_{mn}"],
                                    sa_sb[0:G, 0:ND]).then_inc(so, 16)
                        g.dma_start(t[zsn], zs_sb[0:G, 0:ND]).then_inc(so, 16)
                        g.wait_ge(so, base * 32 + 32)

            with contextlib.ExitStack() as _w:
                qw_sb = _w.enter_context(
                    nc.sbuf_tensor(f"dqqw{mn}", [128, 2 * NP], I32))
                sbs_sb = _w.enter_context(
                    nc.sbuf_tensor(f"dqsb{mn}", [128, 2 * ND], BF16))
                nb_sb = _w.enter_context(
                    nc.sbuf_tensor(f"dqnb{mn}", [128, ND], I32))
                w_sb = _w.enter_context(
                    nc.sbuf_tensor(f"dqw{mn}", [128, 2 * ND], BF16))
                qwsrc = r32(t[qwn])
                with nc.Block() as blk2:
                    @blk2.gpsimd
                    def _(g, G=G, NP=NP, ND=ND, qwsrc=qwsrc, mn=mn,
                          base_l2=l2n[0], base_v2=v2n[0], qw_sb=qw_sb,
                          sbs_sb=sbs_sb):
                        sa = t[f"sa_{mn}"]
                        for kt in range(G):
                            par = kt % 2
                            if kt >= 2:
                                g.wait_ge(ve2, base_v2 + kt - 1)
                            g.dma_start(qw_sb[:, par * NP:par * NP + NP],
                                        qwsrc[kt]).then_inc(ld2, 16)
                            g.dma_start(
                                sbs_sb[:, par * ND:par * ND + ND],
                                bcast_row_off(sa, kt * ND, 128, ND),
                            ).then_inc(ld2, 16)

                    @blk2.vector
                    def _(v, G=G, NP=NP, ND=ND, base_l2=l2n[0],
                          base_v2=v2n[0], base_s2=s2n[0], base_dv=v2n[0],
                          qw_sb=qw_sb, sbs_sb=sbs_sb, nb_sb=nb_sb, w_sb=w_sb):
                        for kt in range(G):
                            par = kt % 2
                            v.wait_ge(ld2, base_l2 + 32 * (kt + 1))
                            if kt >= 1:
                                v.wait_ge(ve2, base_v2 + kt)
                            if kt >= 2:
                                v.wait_ge(so2, base_s2 + 16 * (kt - 1))
                            qw = qw_sb[:, par * NP:par * NP + NP]
                            for i in range(7):
                                v.tensor_single_scalar(
                                    nb_sb[:, TPOS[i]:ND:8], qw, 15 << (4 * i),
                                    ALU.bitwise_and)
                            v.tensor_single_scalar(
                                nb_sb[:, TPOS[7]:ND:8], qw, 28,
                                ALU.logical_shift_right).then_inc(dvq, 1)
                            v.wait_ge(dvq, base_dv + kt + 1)
                            v.tensor_tensor(
                                w_sb[:, par * ND:par * ND + ND], nb_sb[:, 0:ND],
                                sbs_sb[:, par * ND:par * ND + ND],
                                ALU.mult).then_inc(ve2, 1)

                    @blk2.sync
                    def _(sy, G=G, ND=ND, wdn=wdn, base_v2=v2n[0],
                          base_s2=s2n[0], w_sb=w_sb):
                        for kt in range(G):
                            par = kt % 2
                            sy.wait_ge(ve2, base_v2 + kt + 1)
                            sy.dma_start(
                                t[wdn][:, kt].rearrange("n p f -> p n f"),
                                sbdma(w_sb, 128, ND // 128, par * ND),
                            ).then_inc(so2, 16)
                        sy.wait_ge(so2, base_s2 + 16 * G)

            mi[0] += 1
            l2n[0] += 32 * G
            v2n[0] += G
            s2n[0] += 16 * G


def ffn1_builder(nc, t):
    with contextlib.ExitStack() as _es:
        lh = _es.enter_context(nc.semaphore("f1_lh"))
        lw = _es.enter_context(nc.semaphore("f1_lw"))
        lz = _es.enter_context(nc.semaphore("f1_lz"))
        mm = _es.enter_context(nc.semaphore("f1_mm"))
        cm = _es.enter_context(nc.semaphore("f1_cm"))
        sc = _es.enter_context(nc.semaphore("f1_sc"))
        ve = _es.enter_context(nc.semaphore("f1_ve"))
        so = _es.enter_context(nc.semaphore("f1_so"))
        lh2 = _es.enter_context(nc.semaphore("f1_lh2"))
        lw2 = _es.enter_context(nc.semaphore("f1_lw2"))
        mm2 = _es.enter_context(nc.semaphore("f1_mm2"))
        cm2 = _es.enter_context(nc.semaphore("f1_cm2"))
        sc2 = _es.enter_context(nc.semaphore("f1_sc2"))
        so2 = _es.enter_context(nc.semaphore("f1_so2"))

        with contextlib.ExitStack() as _a:
            xn_sb = _a.enter_context(
                nc.sbuf_tensor("f1_xn", [128, NT * T], BF16))
            wg_sb = _a.enter_context(
                nc.sbuf_tensor("f1_wg", [128, 2 * NT * 128], BF16))
            wu_sb = _a.enter_context(
                nc.sbuf_tensor("f1_wu", [128, 2 * NT * 128], BF16))
            ix_sb = _a.enter_context(
                nc.sbuf_tensor("f1_ix", [128, NT * NT], BF16))
            cx_sb = _a.enter_context(
                nc.sbuf_tensor("f1_cx", [128, NCH * 512], BF16))
            zg_sb = _a.enter_context(
                nc.sbuf_tensor("f1_zg", [128, 2 * 128], BF16))
            zu_sb = _a.enter_context(
                nc.sbuf_tensor("f1_zu", [128, 2 * 128], BF16))
            sl_sb = _a.enter_context(
                nc.sbuf_tensor("f1_sl", [128, 2 * 512], BF16))
            h_sb = _a.enter_context(
                nc.sbuf_tensor("f1_h", [128, 2 * 512], BF16))
            pg0 = _a.enter_context(nc.psum_tensor("f1_pg0", [128, 512], F32))
            pg1 = _a.enter_context(nc.psum_tensor("f1_pg1", [128, 512], F32))
            pu0 = _a.enter_context(nc.psum_tensor("f1_pu0", [128, 512], F32))
            pu1 = _a.enter_context(nc.psum_tensor("f1_pu1", [128, 512], F32))
            pcx = _a.enter_context(nc.psum_tensor("f1_pcx", [NT, 512], F32))
            pgl = [pg0, pg1]
            pul = [pu0, pu1]
            ixv = sb3(ix_sb, NT)
            slv = sb3(sl_sb, 512)
            hv = sb3(h_sb, 512)

            def xn_t(k, ch):
                return xn_sb[:, T * k + 512 * ch: T * k + 512 * ch + 512]

            def wslice(wsb, par, k):
                off = (par * NT + k) * 128
                return wsb[:, off:off + 128]

            def sc_cx(ch):
                return ch + 1

            def sc_silu(it):
                return NCH + it + 1

            with nc.Block() as blk:
                @blk.sync
                def _(sy):
                    sy.dma_start(sbdma_full(xn_sb, T),
                                 r32p(t["xnT"])).then_inc(lh, 16)
                    sy.dma_start(sbdma_full(ix_sb, NT),
                                 r32p(t["indX"])).then_inc(lh, 16)
                    for nt in range(ITL):
                        par = nt % 2
                        if nt >= 2:
                            sy.wait_ge(mm, (nt - 1) * NCH)
                        sy.dma_start(zg_sb[0:NT, 128 * par:128 * par + 128],
                                     t["zsg"][0:NT, 128 * nt:128 * nt + 128]
                                     ).then_inc(lz, 16)
                        sy.dma_start(zu_sb[0:NT, 128 * par:128 * par + 128],
                                     t["zsu"][0:NT, 128 * nt:128 * nt + 128]
                                     ).then_inc(lz, 16)
                        if nt >= 1:
                            for ch in range(NCH):
                                it = (nt - 1) * NCH + ch
                                sy.wait_ge(ve, it + 1)
                                sy.dma_start(
                                    r32(t["hT"])[nt - 1, :,
                                                 512 * ch:512 * ch + 512],
                                    hv[it % 2]).then_inc(so, 16)
                    for ch in range(NCH):
                        it = (ITL - 1) * NCH + ch
                        sy.wait_ge(ve, it + 1)
                        sy.dma_start(r32(t["hT"])[ITL - 1, :,
                                                  512 * ch:512 * ch + 512],
                                     hv[it % 2]).then_inc(so, 16)
                    sy.wait_ge(so, 16 * ITL * NCH)

                @blk.gpsimd
                def _(g):
                    for nt in range(ITL):
                        if nt >= 2:
                            g.wait_ge(mm, (nt - 1) * NCH)
                        g.dma_start(
                            sbdma(wg_sb, 128, NT, (nt % 2) * NT * 128),
                            t["wdg"][nt].rearrange("g p f -> p g f")
                        ).then_inc(lw, 16)
                        g.dma_start(
                            sbdma(wu_sb, 128, NT, (nt % 2) * NT * 128),
                            t["wdu"][nt].rearrange("g p f -> p g f")
                        ).then_inc(lw, 16)

                @blk.tensor
                def _(te):
                    te.wait_ge(lh, 16 * 2)
                    for ch in range(NCH):
                        if ch > 0:
                            te.wait_ge(sc, sc_cx(ch - 1))
                        for k in range(NT):
                            ins = te.matmul(pcx[:], ixv[k], xn_t(k, ch),
                                            start=(k == 0), stop=(k == NT - 1))
                        ins.then_inc(cm, 1)
                    te.wait_ge(sc, NCH)
                    for it in range(ITL * NCH):
                        nt, ch = it // NCH, it % NCH
                        par = it % 2
                        if ch == 0:
                            te.wait_ge(lw, 32 * (nt + 1))
                            te.wait_ge(lz, 32 * (nt + 1))
                        if it >= 2:
                            te.wait_ge(ve, it - 1)
                        zpar = nt % 2
                        for k in range(NT):
                            te.matmul(pgl[par][:], wslice(wg_sb, nt % 2, k),
                                      xn_t(k, ch), start=(k == 0), stop=False)
                        te.matmul(pgl[par][:],
                                  zg_sb[0:NT, 128 * zpar:128 * zpar + 128],
                                  cx_sb[0:NT, 512 * ch:512 * ch + 512],
                                  start=False, stop=True)
                        for k in range(NT):
                            te.matmul(pul[par][:], wslice(wu_sb, nt % 2, k),
                                      xn_t(k, ch), start=(k == 0), stop=False)
                        te.matmul(pul[par][:],
                                  zu_sb[0:NT, 128 * zpar:128 * zpar + 128],
                                  cx_sb[0:NT, 512 * ch:512 * ch + 512],
                                  start=False, stop=True).then_inc(mm, 1)

                @blk.scalar
                def _(s):
                    for ch in range(NCH):
                        s.wait_ge(cm, ch + 1)
                        s.copy(cx_sb[0:NT, 512 * ch:512 * ch + 512],
                               pcx[:]).then_inc(sc, 1)
                    for it in range(ITL * NCH):
                        par = it % 2
                        s.wait_ge(mm, it + 1)
                        if it >= 2:
                            s.wait_ge(ve, it - 1)
                        s.activation(slv[par], pgl[par][:],
                                     AF.Silu).then_inc(sc, 1)

                @blk.vector
                def _(v):
                    for it in range(ITL * NCH):
                        par = it % 2
                        v.wait_ge(sc, sc_silu(it))
                        if it >= 2:
                            v.wait_ge(so, 16 * (it - 1))
                        v.tensor_tensor(hv[par], slv[par], pul[par][:],
                                        ALU.mult).then_inc(ve, 1)

        with contextlib.ExitStack() as _b:
            hs_sb = _b.enter_context(
                nc.sbuf_tensor("f1_hs", [128, ITL * 512], BF16))
            wd_sb = _b.enter_context(
                nc.sbuf_tensor("f1_wd", [128, 2 * ITL * 128], BF16))
            ih_sb = _b.enter_context(
                nc.sbuf_tensor("f1_ih", [128, ITL * ITL], BF16))
            chs_sb = _b.enter_context(
                nc.sbuf_tensor("f1_ch", [128, 512], BF16))
            zd_sb = _b.enter_context(
                nc.sbuf_tensor("f1_zd", [128, D], BF16))
            gv_sb = _b.enter_context(
                nc.sbuf_tensor("f1_gv", [128, 2 * 512], BF16))
            pd0 = _b.enter_context(nc.psum_tensor("f1_pd0", [128, 512], F32))
            pd1 = _b.enter_context(nc.psum_tensor("f1_pd1", [128, 512], F32))
            pch = _b.enter_context(nc.psum_tensor("f1_pch", [ITL, 512], F32))
            pdl = [pd0, pd1]
            ihv = sb3(ih_sb, ITL)
            gvv = sb3(gv_sb, 512)

            def sc2_cx(ch):
                return ch * (NT + 1) + 1

            def sc2_silu(i):
                return (i // NT + 1) + (i + 1)

            with nc.Block() as blk2:
                @blk2.sync
                def _(sy):
                    sy.dma_start(sbdma_full(ih_sb, ITL),
                                 r32p(t["indH"])).then_inc(lh2, 16)
                    sy.dma_start(zd_sb[0:ITL, :], t["zsd"]).then_inc(lh2, 16)
                    for ch in range(NCH):
                        if ch > 0:
                            sy.wait_ge(mm2, ch * NT)
                        sy.dma_start(sbdma_full(hs_sb, 512),
                                     r32p(t["hT"])[:, :, 512 * ch:512 * ch + 512]
                                     ).then_inc(lh2, 16)
                        for nt in range(NT):
                            i = ch * NT + nt
                            sy.wait_ge(sc2, sc2_silu(i))
                            sy.dma_start(
                                r32(t["gT"])[nt, :, 512 * ch:512 * ch + 512],
                                gvv[i % 2]).then_inc(so2, 16)
                    sy.wait_ge(so2, 16 * NCH * NT)

                @blk2.gpsimd
                def _(g):
                    for i in range(NCH * NT):
                        nt = i % NT
                        if i >= 2:
                            g.wait_ge(mm2, i - 1)
                        g.dma_start(
                            sbdma(wd_sb, 128, ITL, (i % 2) * ITL * 128),
                            t["wdd"][nt].rearrange("g p f -> p g f")
                        ).then_inc(lw2, 16)

                @blk2.tensor
                def _(te):
                    te.wait_ge(lh2, 32)
                    for ch in range(NCH):
                        te.wait_ge(lh2, 16 * (3 + ch))
                        if ch > 0:
                            te.wait_ge(sc2, sc2_cx(ch - 1))
                        for k in range(ITL):
                            ins = te.matmul(pch[:], ihv[k],
                                            hs_sb[:, 512 * k:512 * k + 512],
                                            start=(k == 0),
                                            stop=(k == ITL - 1))
                        ins.then_inc(cm2, 1)
                        te.wait_ge(sc2, sc2_cx(ch))
                        for nt in range(NT):
                            i = ch * NT + nt
                            par = i % 2
                            te.wait_ge(lw2, 16 * (i + 1))
                            if i >= 2:
                                te.wait_ge(sc2, sc2_silu(i - 2))
                            for k in range(ITL):
                                off = ((i % 2) * ITL + k) * 128
                                te.matmul(pdl[par][:], wd_sb[:, off:off + 128],
                                          hs_sb[:, 512 * k:512 * k + 512],
                                          start=(k == 0), stop=False)
                            te.matmul(pdl[par][:],
                                      zd_sb[0:ITL, 128 * nt:128 * nt + 128],
                                      chs_sb[0:ITL, :], start=False,
                                      stop=True).then_inc(mm2, 1)

                @blk2.scalar
                def _(s):
                    for ch in range(NCH):
                        s.wait_ge(cm2, ch + 1)
                        s.copy(chs_sb[0:ITL, :], pch[:]).then_inc(sc2, 1)
                        for nt in range(NT):
                            i = ch * NT + nt
                            par = i % 2
                            s.wait_ge(mm2, i + 1)
                            if i >= 2:
                                s.wait_ge(so2, 16 * (i - 1))
                            s.activation(gvv[par], pdl[par][:],
                                         AF.Silu).then_inc(sc2, 1)


def _at_store(sy, t, vo, so, ov, hh, ch):
    i = hh * NCH + ch
    sy.wait_ge(vo, i + 1)
    sy.dma_start(r32(t["attnT"])[hh, :, 512 * ch:512 * ch + 512],
                 ov[i % 2]).then_inc(so, 16)


def attn_builder(nc, t, layer):
    causal = layer == 1

    def kts_of(ch):
        return list(range(32)) if not causal else list(range(20 + 4 * ch))

    def is_masked(ch, kt):
        return causal and kt >= 4 * ch

    nvis = [len(kts_of(ch)) for ch in range(NCH)]
    ngrp = [(n + 1) // 2 for n in nvis]
    grp_head = sum(ngrp)
    nmask = [sum(1 for kt in kts_of(ch) if is_masked(ch, kt))
             for ch in range(NCH)]

    def mm_scores(h, ch, gi):
        return (h * (grp_head + NCH)
                + sum(ngrp[c] + 1 for c in range(ch)) + gi + 1)

    def mm_pass2(h, ch):
        return (h * (grp_head + NCH)
                + sum(ngrp[c] + 1 for c in range(ch)) + ngrp[ch] + 1)

    def mm_head(h):
        return (h + 1) * (grp_head + NCH)

    def sc_exp(h, ch, gi):
        return h * grp_head + sum(ngrp[:ch]) + gi + 1

    def g_glob(h, ch, gi):
        return h * grp_head + sum(ngrp[:ch]) + gi

    def rc_rb(hh, ch):
        return hh * NCH + ch + 1

    def bm_rb(hh, ch):
        return hh * NCH + ch + 1

    with contextlib.ExitStack() as _es:
        k_sb = _es.enter_context(nc.sbuf_tensor("at_k", [128, 2 * S], BF16))
        q_sb = _es.enter_context(nc.sbuf_tensor("at_q", [128, 2 * T], BF16))
        v_sb = _es.enter_context(nc.sbuf_tensor("at_v", [128, 2 * 4096], BF16))
        e_sb = _es.enter_context(nc.sbuf_tensor("at_e", [128, 32 * 512], BF16))
        cm_sb = _es.enter_context(
            nc.sbuf_tensor("at_cm", [128, (80 if causal else 1) * 512], BF16))
        st_sb = _es.enter_context(nc.sbuf_tensor("at_st", [1, 640], F32))
        rb_sb = _es.enter_context(nc.sbuf_tensor("at_rb", [128, 512], BF16))
        o_sb = _es.enter_context(nc.sbuf_tensor("at_o", [128, 2 * 512], BF16))
        ones_col = _es.enter_context(nc.sbuf_tensor("at_on", [128, 1], BF16))
        ones_row = _es.enter_context(nc.sbuf_tensor("at_or", [1, 128], F32))
        pe0 = _es.enter_context(nc.psum_tensor("at_pe0", [128, 1024], F32))
        pe1 = _es.enter_context(nc.psum_tensor("at_pe1", [128, 1024], F32))
        ps_d = _es.enter_context(nc.psum_tensor("at_pd", [1, 512], F32))
        ps_o = _es.enter_context(nc.psum_tensor("at_po", [128, 512], F32))
        ps_r = _es.enter_context(nc.psum_tensor("at_pr", [128, 512], F32))
        lc = _es.enter_context(nc.semaphore("at_lc"))
        lhs = _es.enter_context(nc.semaphore("at_lh"))
        mm = _es.enter_context(nc.semaphore("at_mm"))
        bm = _es.enter_context(nc.semaphore("at_bm"))
        sc = _es.enter_context(nc.semaphore("at_sc"))
        rc = _es.enter_context(nc.semaphore("at_rc"))
        ve = _es.enter_context(nc.semaphore("at_ve"))
        vo = _es.enter_context(nc.semaphore("at_vo"))
        so = _es.enter_context(nc.semaphore("at_so"))
        pel = [pe0, pe1]
        ev = sb3(e_sb, 512)
        ov = sb3(o_sb, 512)
        kfull = t["kfull"]
        vfull = r32p(t["vfull"])
        qT = t["qT"]
        RD = slice(16, 528)
        nconst = 3 if causal else 2

        with nc.Block() as blk:
            @blk.sync
            def _(sy):
                sy.dma_start(ones_col[:], t["ones_col"]).then_inc(lc, 16)
                sy.dma_start(ones_row[:], t["ones_row_f32"]).then_inc(lc, 16)
                if causal:
                    sy.dma_start(sbdma_full(cm_sb, 512),
                                 r32p(t["cmask"])).then_inc(lc, 16)

                def store(hh, ch):
                    i = hh * NCH + ch
                    sy.wait_ge(vo, i + 1)
                    sy.dma_start(
                        r32(t["attnT"])[hh, :, 512 * ch:512 * ch + 512],
                        ov[i % 2]).then_inc(so, 16)

                for h in range(HEADS):
                    par = h % 2
                    if h >= 2:
                        sy.wait_ge(mm, mm_head(h - 2))
                    sy.dma_start(k_sb[:, par * S:par * S + T],
                                 kfull[128 * h:128 * h + 128, :]
                                 ).then_inc(lhs, 16)
                    sy.dma_start(k_sb[:, par * S + T:par * S + 2 * T],
                                 kfull[D + 128 * h:D + 128 * h + 128, :]
                                 ).then_inc(lhs, 16)
                    sy.dma_start(
                        sbdma_range(v_sb, 128, par * 4096, par * 4096 + 4096),
                        vfull[:, :, 128 * h:128 * h + 128]).then_inc(lhs, 16)
                    sy.dma_start(q_sb[:, par * T:par * T + T],
                                 qT[128 * h:128 * h + 128, :]).then_inc(lhs, 16)
                    if h >= 1:
                        for ch in range(NCH):
                            store(h - 1, ch)
                for ch in range(NCH):
                    store(HEADS - 1, ch)
                sy.wait_ge(so, 16 * HEADS * NCH)

            @blk.vector
            def _(v):
                v.wait_ge(lc, 16 * nconst)
                for hh in range(HEADS):
                    for ch in range(NCH):
                        i = hh * NCH + ch
                        v.wait_ge(sc, sc_exp(hh, ch, ngrp[ch] - 1))
                        if causal:
                            base = sum(nmask[:ch])
                            ins = None
                            for kt in kts_of(ch):
                                if is_masked(ch, kt):
                                    moff = (base + (kt - 4 * ch)) * 512
                                    ins = v.tensor_tensor(
                                        ev[kt], ev[kt],
                                        cm_sb[:, moff:moff + 512], ALU.mult)
                            ins.then_inc(ve, 1)
                        else:
                            v.nop().then_inc(ve, 1)
                        v.wait_ge(mm, mm_pass2(hh, ch))
                        if i > 0:
                            pch = ((hh, ch - 1) if ch > 0
                                   else (hh - 1, NCH - 1))
                            v.wait_ge(bm, bm_rb(*pch))
                        v.reciprocal(st_sb[0:1, RD], ps_d[:]).then_inc(ve, 1)
                        v.wait_ge(rc, rc_rb(hh, ch))
                        if i >= 2:
                            v.wait_ge(so, 16 * (i - 1))
                        v.tensor_tensor(ov[i % 2], ps_o[:], rb_sb[:],
                                        ALU.mult).then_inc(vo, 1)

            @blk.scalar
            def _(s):
                for hh in range(HEADS):
                    for ch in range(NCH):
                        i = hh * NCH + ch
                        for gi in range(ngrp[ch]):
                            g2 = min(2, nvis[ch] - 2 * gi)
                            s.wait_ge(mm, mm_scores(hh, ch, gi))
                            s.activation(
                                e_sb[:, 512 * 2 * gi:512 * (2 * gi + g2)],
                                pel[g_glob(hh, ch, gi) % 2][:, 0:512 * g2],
                                AF.Exp, bias=0.0,
                                scale=SCALE_ATT).then_inc(sc, 1)
                        s.wait_ge(bm, bm_rb(hh, ch))
                        s.wait_ge(vo, i)
                        s.copy(rb_sb[:], ps_r[:]).then_inc(rc, 1)

            @blk.tensor
            def _(te):
                te.wait_ge(lc, 16 * nconst)
                for hh in range(HEADS):
                    parh = hh % 2
                    te.wait_ge(lhs, 64 * (hh + 1))
                    for ch in range(NCH):
                        i = hh * NCH + ch
                        for gi in range(ngrp[ch]):
                            G = g_glob(hh, ch, gi)
                            if G >= 2:
                                te.wait_ge(sc, G - 1)
                            for j in (0, 1):
                                kt = 2 * gi + j
                                if kt >= nvis[ch]:
                                    continue
                                ins = te.matmul(
                                    pel[G % 2][:, 512 * j:512 * j + 512],
                                    k_sb[:, parh * S + 128 * kt:
                                         parh * S + 128 * kt + 128],
                                    q_sb[:, parh * T + 512 * ch:
                                         parh * T + 512 * ch + 512],
                                    start=True, stop=True)
                            ins.then_inc(mm, 1)
                        te.wait_ge(sc, sc_exp(hh, ch, ngrp[ch] - 1))
                        te.wait_ge(ve, 2 * i + 1)
                        if i >= 1:
                            te.wait_ge(vo, i)
                        for vi, kt in enumerate(kts_of(ch)):
                            te.matmul(ps_d[:], ones_col[:], ev[kt],
                                      start=(vi == 0),
                                      stop=(vi == nvis[ch] - 1))
                            ins = te.matmul(
                                ps_o[:],
                                v_sb[:, parh * 4096 + 128 * kt:
                                     parh * 4096 + 128 * kt + 128],
                                ev[kt], start=(vi == 0),
                                stop=(vi == nvis[ch] - 1))
                        ins.then_inc(mm, 1)
                        te.wait_ge(ve, 2 * i + 2)
                        if i >= 1:
                            pch = ((hh, ch - 1) if ch > 0
                                   else (hh - 1, NCH - 1))
                            te.wait_ge(rc, rc_rb(*pch))
                        te.matmul(ps_r[:], ones_row[:], st_sb[0:1, RD],
                                  start=True, stop=True).then_inc(bm, 1)


def ogr_builder(nc, t):
    with contextlib.ExitStack() as _es:
        a_sb = _es.enter_context(nc.sbuf_tensor("og_a", [128, NT * T], BF16))
        w_sb = _es.enter_context(nc.sbuf_tensor("og_w", [128, 2 * NT * 128], BF16))
        s_sb = _es.enter_context(nc.sbuf_tensor("og_s", [128, 2 * 512], BF16))
        g_sb = _es.enter_context(nc.sbuf_tensor("og_g", [128, 2 * 512], BF16))
        x_sb = _es.enter_context(nc.sbuf_tensor("og_x", [128, 2 * 512], F32))
        p_sb = _es.enter_context(nc.sbuf_tensor("og_p", [128, 512], F32))
        o_sb = _es.enter_context(nc.sbuf_tensor("og_o", [128, 2 * 512], F32))
        ps0 = _es.enter_context(nc.psum_tensor("og_p0", [128, 512], F32))
        ps1 = _es.enter_context(nc.psum_tensor("og_p1", [128, 512], F32))
        la = _es.enter_context(nc.semaphore("og_la"))
        lw = _es.enter_context(nc.semaphore("og_lw"))
        lg = _es.enter_context(nc.semaphore("og_lg"))
        mm = _es.enter_context(nc.semaphore("og_mm"))
        sc = _es.enter_context(nc.semaphore("og_sc"))
        ve = _es.enter_context(nc.semaphore("og_ve"))
        so = _es.enter_context(nc.semaphore("og_so"))
        vp = _es.enter_context(nc.semaphore("og_vp"))
        blk = _es.enter_context(nc.Block())
        wv = sb3(w_sb, 128)
        sv = sb3(s_sb, 512)
        gv = sb3(g_sb, 512)
        xv = sb3(x_sb, 512)
        ov = sb3(o_sb, 512)
        psl = [ps0, ps1]

        def a_t(k, ch):
            return a_sb[:, T * k + 512 * ch: T * k + 512 * ch + 512]

        @blk.sync
        def _(sy):
            sy.dma_start(sbdma_full(a_sb, T),
                         r32p(t["attnT"])).then_inc(la, 16)

            def store(j):
                nt, ch = j // NCH, j % NCH
                sy.wait_ge(ve, j + 1)
                sy.dma_start(r32(t["xoutT"])[nt, :, 512 * ch:512 * ch + 512],
                             ov[j % 2]).then_inc(so, 16)

            for it in range(NT * NCH):
                par = it % 2
                nt, ch = it // NCH, it % NCH
                if it >= 2:
                    sy.wait_ge(ve, it - 1)
                sy.dma_start(gv[par],
                             r32(t["gT"])[nt, :, 512 * ch:512 * ch + 512]
                             ).then_inc(lg, 16)
                sy.dma_start(xv[par],
                             r32(t["xT"])[nt, :, 512 * ch:512 * ch + 512]
                             ).then_inc(lg, 16)
                if it >= 1:
                    store(it - 1)
            store(NT * NCH - 1)
            sy.wait_ge(so, 16 * NT * NCH)

        @blk.gpsimd
        def _(g):
            wsrc = r32p(t["woT"])
            for nt in range(NT):
                if nt >= 2:
                    g.wait_ge(mm, (nt - 1) * NCH)
                g.dma_start(
                    sbdma(w_sb, 128, NT, (nt % 2) * NT * 128),
                    wsrc[:, :, 128 * nt:128 * nt + 128],
                ).then_inc(lw, 16)

        @blk.tensor
        def _(te):
            te.wait_ge(la, 16)
            for it in range(NT * NCH):
                nt, ch = it // NCH, it % NCH
                par = it % 2
                if ch == 0:
                    te.wait_ge(lw, 16 * (nt + 1))
                if it >= 2:
                    te.wait_ge(sc, it - 1)
                for k in range(NT):
                    ins = te.matmul(psl[par][:], wv[(nt % 2) * NT + k],
                                    a_t(k, ch), start=(k == 0),
                                    stop=(k == NT - 1))
                ins.then_inc(mm, 1)

        @blk.scalar
        def _(s):
            for it in range(NT * NCH):
                par = it % 2
                s.wait_ge(mm, it + 1)
                if it >= 2:
                    s.wait_ge(ve, it - 1)
                s.copy(sv[par], psl[par][:]).then_inc(sc, 1)

        @blk.vector
        def _(v):
            for it in range(NT * NCH):
                par = it % 2
                v.wait_ge(sc, it + 1)
                v.wait_ge(lg, 32 * (it + 1))
                if it >= 2:
                    v.wait_ge(so, 16 * (it - 1))
                if it >= 1:
                    v.wait_ge(ve, it)
                v.tensor_tensor(p_sb[:], sv[par], gv[par],
                                ALU.mult).then_inc(vp, 1)
                v.wait_ge(vp, it + 1)
                v.tensor_tensor(ov[par], p_sb[:], xv[par],
                                ALU.add).then_inc(ve, 1)


def _cc(a):
    return np.ascontiguousarray(a)


def _prep_shared(inp):
    P = {}
    P["wqT0"] = _cc(inp["L0_attn_w_q"].T)
    P["wkT0"] = _cc(inp["L0_attn_w_k"].T)
    P["wvT0"] = _cc(inp["L0_attn_w_v"].T)
    P["woT0"] = _cc(inp["L0_attn_w_o"].T)
    P["W0T"] = _cc(inp["L0_ffn_W"].T)
    P["b0rs"] = _cc(inp["L0_ffn_b"].reshape(NT, 128).T)
    P["A_T"] = _cc(inp["L0_ffn_A"].reshape(128, D).T)
    P["C0"] = _cc(inp["L0_ffn_B"].transpose(0, 2, 1).reshape(128, D))
    P["rwT"] = _cc(inp["L0_ffn_router_W"].T)
    P["rb"] = _cc(inp["L0_ffn_router_b"].reshape(1, 8))
    P["wqT1"] = _cc(inp["L1_attn_wq"].T)
    P["wkT1"] = _cc(inp["L1_attn_wk"].T)
    P["wvT1"] = _cc(inp["L1_attn_wv"].T)
    P["woT1"] = _cc(inp["L1_attn_wo"].T)
    P["gqw"] = _cc(inp["L1_ffn_gate_qweight"])
    P["gsc"] = _cc(inp["L1_ffn_gate_scales"])
    P["gqz"] = _cc(inp["L1_ffn_gate_qzeros"])
    P["uqw"] = _cc(inp["L1_ffn_up_qweight"])
    P["usc"] = _cc(inp["L1_ffn_up_scales"])
    P["uqz"] = _cc(inp["L1_ffn_up_qzeros"])
    P["dqw"] = _cc(inp["L1_ffn_down_qweight"])
    P["dsc"] = _cc(inp["L1_ffn_down_scales"])
    P["dqz"] = _cc(inp["L1_ffn_down_qzeros"])

    P["ones_col"] = np.ones((128, 1), BF)
    P["ones_row_f32"] = np.ones((1, 128), np.float32)
    jt = np.zeros((128, 128), np.float32)
    for j in range(64):
        jt[2 * j, 2 * j + 1] = 1.0
        jt[2 * j + 1, 2 * j] = -1.0
    P["JT"] = jt.astype(BF)

    def pow2row(n):
        r = np.zeros(n, np.float32)
        for j in range(n // 8):
            for i in range(8):
                r[8 * j + TPOS[i]] = 1.0 if i == 7 else 2.0 ** (-4 * i)
        return r.reshape(1, n)

    P["p2g"] = pow2row(IDIM)
    P["p2d"] = pow2row(D)
    P["indX"] = np.repeat(np.eye(NT, dtype=np.float32), 128, axis=0).astype(BF)
    P["indH"] = np.repeat(np.eye(ITL, dtype=np.float32), 128, axis=0).astype(BF)
    rrep = np.zeros((8, 128), np.float32)
    for e in range(8):
        rrep[e, 16 * e:16 * e + 16] = 1.0
    P["Rrep"] = rrep.astype(BF)
    return P


def _prep_core(inp, c):
    b, hh = c // 2, c % 2
    Q = {}
    x = np.asarray(inp["L0_attn_x"])
    Q["xT"] = _cc(x[b].T[:, T * hh:T * hh + T]).astype(np.float32)
    pos = np.arange(T * hh, T * hh + T, dtype=np.float32)
    Q["cosb"] = np.broadcast_to(np.cos(pos)[None, :], (128, T)).astype(BF)
    Q["sinb"] = np.broadcast_to(np.sin(pos)[None, :], (128, T)).astype(BF)
    m = ((pos >= 611) & (pos <= 4094)).astype(np.float32)
    Q["maskln"] = np.broadcast_to(m[None, :], (128, T)).astype(BF)
    cm = np.zeros((NCH, 20, 128, 512), np.float32)
    p = np.arange(128)[:, None]
    f = np.arange(512)[None, :]
    for ch in range(NCH):
        for j in range(20):
            kt = 4 * ch + j
            cm[ch, j] = ((128 * kt + p) <= (T * hh + 512 * ch + f))
    Q["cmask"] = cm.reshape(NCH * 20 * 128, 512).astype(BF)
    return Q


def _specs_ln(layer):
    s = {"xT": ([D, T], F32, "ExternalInput"),
         "ones_col": ([128, 1], BF16, "ExternalInput"),
         "ones_row_f32": ([1, 128], F32, "ExternalInput"),
         "xnT": ([D, T], BF16, "ExternalOutput")}
    if layer == 0:
        s["maskln"] = ([128, T], BF16, "ExternalInput")
        s["msum"] = ([128, NT], F32, "ExternalOutput")
    return s


def _specs_qkv(layer):
    s = {"xnT": ([D, T], BF16, "ExternalInput"),
         "wqT": ([D, D], F32, "ExternalInput"),
         "wkT": ([D, D], F32, "ExternalInput"),
         "wvT": ([D, D], F32, "ExternalInput"),
         "qT": ([D, T], BF16, "ExternalOutput"),
         "kown": ([D, T], BF16, "ExternalOutput"),
         "vown": ([T, D], BF16, "ExternalOutput")}
    if layer == 1:
        s["JT"] = ([128, 128], BF16, "ExternalInput")
        s["cosb"] = ([128, T], BF16, "ExternalInput")
        s["sinb"] = ([128, T], BF16, "ExternalInput")
    return s


_SPECS_FFN0 = {
    "xnT": ([D, T], BF16, "ExternalInput"),
    "msum_a": ([128, NT], F32, "ExternalInput"),
    "msum_b": ([128, NT], F32, "ExternalInput"),
    "rwT": ([D, 8], F32, "ExternalInput"),
    "rb": ([1, 8], F32, "ExternalInput"),
    "b0rs": ([128, NT], F32, "ExternalInput"),
    "A_T": ([D, 128], F32, "ExternalInput"),
    "C0": ([128, D], F32, "ExternalInput"),
    "W0T": ([D, D], F32, "ExternalInput"),
    "rt8": ([1, 8], F32, "Internal"),
    "gT": ([D, T], BF16, "ExternalOutput"),
}

_SPECS_DEQ = {
    "gqw": ([D, IDIM // 8], I32, "ExternalInput"),
    "gsc": ([NT, IDIM], F32, "ExternalInput"),
    "gqz": ([NT, IDIM // 8], I32, "ExternalInput"),
    "uqw": ([D, IDIM // 8], I32, "ExternalInput"),
    "usc": ([NT, IDIM], F32, "ExternalInput"),
    "uqz": ([NT, IDIM // 8], I32, "ExternalInput"),
    "dqw": ([IDIM, D // 8], I32, "ExternalInput"),
    "dsc": ([ITL, D], F32, "ExternalInput"),
    "dqz": ([ITL, D // 8], I32, "ExternalInput"),
    "sa_g": ([NT, IDIM], BF16, "Internal"),
    "sa_u": ([NT, IDIM], BF16, "Internal"),
    "sa_d": ([ITL, D], BF16, "Internal"),
    "wdg": ([ITL, NT, 128, 128], BF16, "ExternalOutput"),
    "wdu": ([ITL, NT, 128, 128], BF16, "ExternalOutput"),
    "wdd": ([NT, ITL, 128, 128], BF16, "ExternalOutput"),
    "zsg": ([NT, IDIM], BF16, "ExternalOutput"),
    "zsu": ([NT, IDIM], BF16, "ExternalOutput"),
    "zsd": ([ITL, D], BF16, "ExternalOutput"),
}

_SPECS_FFN1 = {
    "xnT": ([D, T], BF16, "ExternalInput"),
    "wdg": ([ITL, NT, 128, 128], BF16, "ExternalInput"),
    "wdu": ([ITL, NT, 128, 128], BF16, "ExternalInput"),
    "wdd": ([NT, ITL, 128, 128], BF16, "ExternalInput"),
    "zsg": ([NT, IDIM], BF16, "ExternalInput"),
    "zsu": ([NT, IDIM], BF16, "ExternalInput"),
    "zsd": ([ITL, D], BF16, "ExternalInput"),
    "indX": ([D, NT], BF16, "ExternalInput"),
    "indH": ([IDIM, ITL], BF16, "ExternalInput"),
    "hT": ([IDIM, T], BF16, "Internal"),
    "gT": ([D, T], BF16, "ExternalOutput"),
}


def _specs_attn(layer):
    s = {"qT": ([D, T], BF16, "ExternalInput"),
         "kfull": ([2 * D, T], BF16, "ExternalInput"),
         "vfull": ([S, D], BF16, "ExternalInput"),
         "ones_col": ([128, 1], BF16, "ExternalInput"),
         "ones_row_f32": ([1, 128], F32, "ExternalInput"),
         "attnT": ([D, T], BF16, "ExternalOutput")}
    if layer == 1:
        s["cmask"] = ([NCH * 20 * 128, 512], BF16, "ExternalInput")
    return s


_SPECS_OGR = {
    "attnT": ([D, T], BF16, "ExternalInput"),
    "woT": ([D, D], F32, "ExternalInput"),
    "gT": ([D, T], BF16, "ExternalInput"),
    "xT": ([D, T], F32, "ExternalInput"),
    "xoutT": ([D, T], F32, "ExternalOutput"),
}

_NP_OF = {F32: np.float32, BF16: BF, I32: np.int32}


def _launch(builder, bargs, specs, inmaps):
    nc = bass.Bass("TRN2", target_bir_lowering=False)
    aps = {}
    for name, (shape, dt, kind) in specs.items():
        aps[name] = nc.dram_tensor(name, shape, dt, kind=kind).ap()
    builder(nc, aps, *bargs)
    res = run_bass_kernel_spmd(nc, inmaps, core_ids=list(range(8)))
    return res.results


def _maps(specs, *dicts):
    out = []
    for c in range(8):
        m = {}
        for name, (shape, dt, kind) in specs.items():
            if kind != "ExternalInput":
                continue
            v = None
            for d in dicts:
                if name in d:
                    v = d[name]
                    break
            if v is None:
                raise KeyError(name)
            a = v[c] if isinstance(v, list) else v
            assert list(a.shape) == list(shape), (name, a.shape, shape)
            m[name] = a
        out.append(m)
    return out


def kernel(**inputs):
    import os
    dbg = os.environ.get("KDBG")
    P = _prep_shared(inputs)
    Q = [_prep_core(inputs, c) for c in range(8)]
    consts = {k: P[k] for k in P}

    def pc(name):
        return [q[name] for q in Q]

    xT = pc("xT")
    out = np.empty((4, S, D), np.float32)

    for layer in (0, 1):
        sp = _specs_ln(layer)
        r = _launch(ln_builder, (layer,), sp,
                    _maps(sp, {"xT": xT, "maskln": pc("maskln")}, consts))
        xn = [r[c]["xnT"] for c in range(8)]
        if layer == 0:
            msum = [r[c]["msum"] for c in range(8)]

        sp = _specs_qkv(layer)
        wq, wk, wv = (P[f"wqT{layer}"], P[f"wkT{layer}"], P[f"wvT{layer}"])
        r = _launch(qkv_builder, (layer,), sp,
                    _maps(sp, {"xnT": xn, "wqT": wq, "wkT": wk, "wvT": wv,
                               "cosb": pc("cosb"), "sinb": pc("sinb")},
                          consts))
        qTl = [r[c]["qT"] for c in range(8)]
        kown = [r[c]["kown"] for c in range(8)]
        vown = [r[c]["vown"] for c in range(8)]
        kfull = [np.concatenate([kown[2 * (c // 2)], kown[2 * (c // 2) + 1]],
                                axis=0) for c in range(8)]
        vfull = [np.concatenate([vown[2 * (c // 2)], vown[2 * (c // 2) + 1]],
                                axis=0) for c in range(8)]

        if layer == 0:
            sp = _SPECS_FFN0
            ma = [msum[2 * (c // 2)] for c in range(8)]
            mb = [msum[2 * (c // 2) + 1] for c in range(8)]
            r = _launch(ffn0_builder, (), sp,
                        _maps(sp, {"xnT": xn, "msum_a": ma, "msum_b": mb},
                              consts))
            gT = [r[c]["gT"] for c in range(8)]
        else:
            sp = _SPECS_DEQ
            r = _launch(deq_builder, (), sp, _maps(sp, consts))
            deq_out = {k: [r[c][k] for c in range(8)]
                       for k in ("wdg", "wdu", "wdd", "zsg", "zsu", "zsd")}
            sp = _SPECS_FFN1
            r = _launch(ffn1_builder, (), sp,
                        _maps(sp, {"xnT": xn, **deq_out}, consts))
            gT = [r[c]["gT"] for c in range(8)]

        if dbg:
            np.save(f"/tmp/kdbg_xn{layer}.npy",
                    np.stack([np.asarray(a).view(np.uint16) for a in xn]))
            np.save(f"/tmp/kdbg_g{layer}.npy",
                    np.stack([np.asarray(a).view(np.uint16) for a in gT]))
            np.save(f"/tmp/kdbg_q{layer}.npy",
                    np.stack([np.asarray(a).view(np.uint16) for a in qTl]))
            np.save(f"/tmp/kdbg_k{layer}.npy",
                    np.stack([np.asarray(a).view(np.uint16) for a in kown]))
            np.save(f"/tmp/kdbg_v{layer}.npy",
                    np.stack([np.asarray(a).view(np.uint16) for a in vown]))
        sp = _specs_attn(layer)
        r = _launch(attn_builder, (layer,), sp,
                    _maps(sp, {"qT": qTl, "kfull": kfull, "vfull": vfull,
                               "cmask": pc("cmask")}, consts))
        attnT = [r[c]["attnT"] for c in range(8)]

        sp = _SPECS_OGR
        r = _launch(ogr_builder, (), sp,
                    _maps(sp, {"attnT": attnT, "woT": P[f"woT{layer}"],
                               "gT": gT, "xT": xT}, consts))
        xT = [np.asarray(r[c]["xoutT"]) for c in range(8)]
        if dbg:
            np.save(f"/tmp/kdbg_attn{layer}.npy",
                    np.stack([np.asarray(a).view(np.uint16) for a in attnT]))
            np.save(f"/tmp/kdbg_x{layer + 1}.npy", np.stack(xT))

    for c in range(8):
        b, hh = c // 2, c % 2
        out[b, T * hh:T * hh + T, :] = xT[c].T
    return out


RG_PAIRS = [[0, 1], [2, 3], [4, 5], [6, 7]]


class _NCAlias:

    def __init__(self, nc, tag):
        self._nc = nc
        self._tag = tag

    def sbuf_tensor(self, name, *a, **k):
        return self._nc.sbuf_tensor(f"{name}_{self._tag}", *a, **k)

    def psum_tensor(self, name, *a, **k):
        return self._nc.psum_tensor(f"{name}_{self._tag}", *a, **k)

    def semaphore(self, name=None, num=None):
        return self._nc.semaphore(
            name=None if name is None else f"{name}_{self._tag}", num=num)

    def __getattr__(self, x):
        return getattr(self._nc, x)



def _fused_build(nc):
    dr = {}

    def mk(name, shape, dt, kind):
        dr[name] = nc.dram_tensor(name, shape, dt, kind=kind).ap()
        return dr[name]

    ext = {
        "xT": ([D, T], F32), "maskln": ([128, T], BF16),
        "cosb": ([128, T], BF16), "sinb": ([128, T], BF16),
        "cmask": ([NCH * 20 * 128, 512], BF16),
        "ones_col": ([128, 1], BF16), "ones_row_f32": ([1, 128], F32),
        "JT": ([128, 128], BF16),
        "wqT0": ([D, D], F32), "wkT0": ([D, D], F32), "wvT0": ([D, D], F32),
        "woT0": ([D, D], F32), "W0T": ([D, D], F32),
        "b0rs": ([128, NT], F32), "A_T": ([D, 128], F32), "C0": ([128, D], F32),
        "rwT": ([D, 8], F32), "rb": ([1, 8], F32),
        "wqT1": ([D, D], F32), "wkT1": ([D, D], F32), "wvT1": ([D, D], F32),
        "woT1": ([D, D], F32),
        "gqw": ([D, IDIM // 8], I32), "gsc": ([NT, IDIM], F32),
        "gqz": ([NT, IDIM // 8], I32),
        "uqw": ([D, IDIM // 8], I32), "usc": ([NT, IDIM], F32),
        "uqz": ([NT, IDIM // 8], I32),
        "dqw": ([IDIM, D // 8], I32), "dsc": ([ITL, D], F32),
        "dqz": ([ITL, D // 8], I32),
    }
    for n, (shape, dt) in ext.items():
        mk(n, shape, dt, "ExternalInput")
    for layer in (0, 1):
        mk(f"xnT{layer}", [D, T], BF16, "Internal")
        mk(f"qT{layer}", [D, T], BF16, "Internal")
        mk(f"kown{layer}", [D, T], BF16, "Internal")
        mk(f"vown{layer}", [T, D], BF16, "Internal")
        mk(f"kfull{layer}", [2 * D, T], BF16, "Internal")
        mk(f"vfull{layer}", [S, D], BF16, "Internal")
        mk(f"gT{layer}", [D, T], BF16, "Internal")
        mk(f"attnT{layer}", [D, T], BF16, "Internal")
    mk("msum", [128, NT], F32, "Internal")
    mk("msumAG", [256, NT], F32, "Internal")
    mk("rt8", [1, 8], F32, "Internal")
    mk("x1T", [D, T], F32, "Internal")
    for n in ("sa_g", "zsg", "sa_u", "zsu"):
        mk(n, [NT, IDIM], BF16, "Internal")
    for n in ("sa_d", "zsd"):
        mk(n, [ITL, D], BF16, "Internal")
    mk("wdg", [ITL, NT, 128, 128], BF16, "Internal")
    mk("wdu", [ITL, NT, 128, 128], BF16, "Internal")
    mk("wdd", [NT, ITL, 128, 128], BF16, "Internal")
    mk("hT", [IDIM, T], BF16, "Internal")
    mk("indX", [D, NT], BF16, "ExternalInput")
    mk("indH", [IDIM, ITL], BF16, "ExternalInput")
    mk("xoutT", [D, T], F32, "ExternalOutput")

    ccn = [0]

    def cc_block(pairs):
        ccn[0] += 1
        with nc.semaphore(f"cc_s{ccn[0]}") as cs, nc.Block() as blk:
            @blk.gpsimd
            def _(g):
                n = 0
                for src, dst in pairs:
                    g.collective_compute(
                        "AllGather", ALU.bypass, replica_groups=RG_PAIRS,
                        ins=[src.opt()], outs=[dst.opt()]).then_inc(cs, 1)
                    n += 1
                g.wait_ge(cs, n)

    for layer in (0, 1):
        xin = dr["xT"] if layer == 0 else dr["x1T"]
        t_ln = {"xT": xin, "xnT": dr[f"xnT{layer}"], "msum": dr["msum"],
                "maskln": dr["maskln"], "ones_col": dr["ones_col"],
                "ones_row_f32": dr["ones_row_f32"]}
        ln_builder(_NCAlias(nc, f'L{layer}a'), t_ln, layer)
        nc.reset()
        if layer == 0:
            cc_block([(dr["msum"], dr["msumAG"])])
            nc.reset()
        t_qkv = {"xnT": dr[f"xnT{layer}"], "wqT": dr[f"wqT{layer}"],
                 "wkT": dr[f"wkT{layer}"], "wvT": dr[f"wvT{layer}"],
                 "qT": dr[f"qT{layer}"], "kown": dr[f"kown{layer}"],
                 "vown": dr[f"vown{layer}"], "JT": dr["JT"],
                 "cosb": dr["cosb"], "sinb": dr["sinb"]}
        qkv_builder(_NCAlias(nc, f'L{layer}b'), t_qkv, layer)
        nc.reset()
        cc_block([(dr[f"kown{layer}"], dr[f"kfull{layer}"]),
                  (dr[f"vown{layer}"], dr[f"vfull{layer}"])])
        nc.reset()
        if layer == 0:
            t_f = {"xnT": dr["xnT0"], "msum_a": dr["msumAG"][0:128, :],
                   "msum_b": dr["msumAG"][128:256, :], "rwT": dr["rwT"],
                   "rb": dr["rb"], "b0rs": dr["b0rs"], "A_T": dr["A_T"],
                   "C0": dr["C0"], "W0T": dr["W0T"], "rt8": dr["rt8"],
                   "gT": dr["gT0"]}
            ffn0_builder(_NCAlias(nc, 'L0f'), t_f)
            nc.reset()
        else:
            t_d = {k: dr[k] for k in
                   ("gqw", "gsc", "gqz", "uqw", "usc", "uqz", "dqw", "dsc",
                    "dqz", "sa_g", "sa_u", "sa_d", "wdg", "wdu", "wdd",
                    "zsg", "zsu", "zsd")}
            deq_builder(_NCAlias(nc, 'L1d'), t_d)
            nc.reset()
            t_f = {"xnT": dr["xnT1"], "wdg": dr["wdg"], "wdu": dr["wdu"],
                   "wdd": dr["wdd"], "zsg": dr["zsg"], "zsu": dr["zsu"],
                   "zsd": dr["zsd"], "indX": dr["indX"], "indH": dr["indH"],
                   "hT": dr["hT"], "gT": dr["gT1"]}
            ffn1_builder(_NCAlias(nc, 'L1f'), t_f)
            nc.reset()
        t_a = {"qT": dr[f"qT{layer}"], "kfull": dr[f"kfull{layer}"],
               "vfull": dr[f"vfull{layer}"], "cmask": dr["cmask"],
               "ones_col": dr["ones_col"], "ones_row_f32": dr["ones_row_f32"],
               "attnT": dr[f"attnT{layer}"]}
        attn_builder(_NCAlias(nc, f'L{layer}t'), t_a, layer)
        nc.reset()
        t_o = {"attnT": dr[f"attnT{layer}"], "woT": dr[f"woT{layer}"],
               "gT": dr[f"gT{layer}"], "xT": xin,
               "xoutT": dr["x1T"] if layer == 0 else dr["xoutT"]}
        ogr_builder(_NCAlias(nc, f'L{layer}o'), t_o)
        if layer == 0:
            nc.reset()


def _kernel_fused(inputs):
    P = _prep_shared(inputs)
    Q = [_prep_core(inputs, c) for c in range(8)]
    nc = bass.Bass("TRN2", target_bir_lowering=False)
    _fused_build(nc)
    inmaps = []
    for c in range(8):
        m = {}
        for n in ("wqT0", "wkT0", "wvT0", "woT0", "W0T", "b0rs", "A_T", "C0",
                  "rwT", "rb", "wqT1", "wkT1", "wvT1", "woT1", "gqw", "gsc",
                  "gqz", "uqw", "usc", "uqz", "dqw", "dsc", "dqz", "ones_col",
                  "ones_row_f32", "JT", "indX", "indH"):
            m[n] = P[n]
        for n in ("xT", "maskln", "cosb", "sinb", "cmask"):
            m[n] = Q[c][n]
        inmaps.append(m)
    res = run_bass_kernel_spmd(nc, inmaps, core_ids=list(range(8)))
    out = np.empty((4, S, D), np.float32)
    for c in range(8):
        b, hh = c // 2, c % 2
        out[b, T * hh:T * hh + T, :] = np.asarray(res.results[c]["xoutT"]).T
    return out


_KERNEL_UNFUSED = kernel


def kernel(**inputs):
    import os
    if os.environ.get("KUNFUSED"):
        return _KERNEL_UNFUSED(**inputs)
    return _kernel_fused(inputs)
